# version 02 (final)
# baseline (speedup 1.0000x reference)
import numpy as np
import concourse.bass as bass
import concourse.mybir as mybir
from concourse.bass_utils import run_bass_kernel_spmd

F32 = mybir.dt.float32
BF16 = mybir.dt.bfloat16
U8 = mybir.dt.uint8
AF = mybir.ActivationFunctionType
ALU = mybir.AluOpType
AX = mybir.AxisListType.X

P = 128
D = 4096
NF = 32
TOK = 2048
NT = 16
HEADS = 32
DH = 128
BS = 256
NBLK = 8
NCORE = 8
GROUP = 4

KB = 1024
OFF_BIG = 0
OFF_B = 131072
OFF_EST = OFF_B + 50 * KB
OFF_KV = OFF_EST + 8 * KB
OFF_CONST = OFF_KV + 16 * KB

ENGS = ("pe", "act", "dve", "sp", "gp")
SEMS = ("s_gp", "s_sp", "s_spst", "s_actst", "s_pe", "s_act", "s_dve",
        "s_col", "s_out")

PHASES = ("p0", "av", "ak", "aq", "agate", "ffn0", "b", "ffn1", "kvstart",
          "c", "merge", "afinal", "ffn2")


def _decay_consts():
    base = 1.0 / 2.0 ** (8.0 / HEADS)
    factor = 1.0 - 1 / (32 - 1 + 1e-5) + 1e-5
    slope = base ** (np.arange(HEADS, dtype=np.float64) + 1.0) * factor
    bd = np.exp(-slope * BS)
    return slope, bd


class Prog:
    def __init__(self, nc, ctx):
        self.nc = nc
        self.ops = {k: [] for k in ENGS}
        self.sems = {}
        self.cnt = {}
        for s in SEMS:
            self.sems[s] = ctx.enter_context(nc.semaphore(s))
            self.cnt[s] = 0

    def do(self, eng, fn):
        self.ops[eng].append(fn)

    def wait(self, eng, sem, val):
        if val <= 0:
            return
        assert val <= self.cnt[sem], f"wait {sem}>={val} unreachable ({self.cnt[sem]})"
        s = self.sems[sem]
        self.do(eng, lambda e, s=s, v=val: e.wait_ge(s, v))

    def wait_multi(self, eng, conds):
        for sem, val in conds:
            if sem is not None:
                self.wait(eng, sem, val)

    def emit(self, eng, fn, inc=None, amt=1):
        tot = None
        if inc is not None:
            self.cnt[inc] += amt
            tot = self.cnt[inc]
            s = self.sems[inc]
            self.do(eng, lambda e, f=fn, s=s, a=amt: f(e).then_inc(s, a))
        else:
            self.do(eng, lambda e, f=fn: f(e))
        return tot

    def gate(self, engines=ENGS, sems=("s_gp", "s_sp", "s_spst", "s_actst",
                                       "s_pe", "s_act", "s_dve")):
        for eng in engines:
            for s in sems:
                self.wait(eng, s, self.cnt[s])

    def act_warmup(self, func, n=24):
        for _ in range(n):
            self.do("act", lambda e, f=func: e.activation(
                out=self._spare[:, 12:13], in_=self._qd[:, 1:2], func=f))

    def spacer(self, eng, n=6):
        if eng == "pe":
            for _ in range(n):
                self.do("pe", lambda e: e.transpose(
                    self._pt0[:, 0:128], self._ident, self._ident))
        elif eng == "dve":
            for _ in range(2 * n):
                self.do("dve", lambda e: e.tensor_copy(
                    out=self._spare[:, 12:16], in_=self._qd[:, 0:4]))
        elif eng == "act":
            for _ in range(2 * n):
                self.do("act", lambda e: e.copy(
                    out=self._spare[:, 12:16], in_=self._qd[:, 0:4]))


def build_nc(dbg=(), upto=None):
    assert upto is None or upto in PHASES
    nc = bass.Bass("TRN2", target_bir_lowering=False)
    slope, bd = _decay_consts()

    x_in = nc.dram_tensor("x_in", [TOK, D], F32, kind="ExternalInput").ap()
    wq_fm = nc.dram_tensor("wq_fm", [NF, P, D], F32, kind="ExternalInput").ap()
    wk_fm = nc.dram_tensor("wk_fm", [NF, P, D], F32, kind="ExternalInput").ap()
    wv_tm = nc.dram_tensor("wv_tm", [8, 4, P, D], F32, kind="ExternalInput").ap()
    wg_tm = nc.dram_tensor("wg_tm", [8, 4, P, D], F32, kind="ExternalInput").ap()
    wo_tm = nc.dram_tensor("wo_tm", [8, 4, P, D], F32, kind="ExternalInput").ap()
    m0_fm = nc.dram_tensor("m0_fm", [NF, P, D], U8, kind="ExternalInput").ap()
    m1_fm = nc.dram_tensor("m1_fm", [NF, P, D], U8, kind="ExternalInput").ap()
    m2_tm = nc.dram_tensor("m2_tm", [8, 4, P, D], U8, kind="ExternalInput").ap()
    consts_in = nc.dram_tensor("consts", [P, 872], F32, kind="ExternalInput").ap()
    sc_in = nc.dram_tensor("sc", [1, 3], F32, kind="ExternalInput").ap()

    out_x = nc.dram_tensor("out_x", [TOK, D], F32, kind="ExternalOutput").ap()

    q_s = nc.dram_tensor("q_s", [HEADS, P, TOK], BF16, kind="Internal").ap()
    k_s = nc.dram_tensor("k_s", [HEADS, P, TOK], BF16, kind="Internal").ap()
    kt_s = nc.dram_tensor("kt_s", [TOK, HEADS, DH], BF16, kind="Internal").ap()
    v_s = nc.dram_tensor("v_s", [TOK, HEADS, DH], BF16, kind="Internal").ap()
    gate_s = nc.dram_tensor("gate_s", [TOK, D], F32, kind="Internal").ap()
    h0_s = nc.dram_tensor("h0_s", [D, TOK], BF16, kind="Internal").ap()
    h1_s = nc.dram_tensor("h1_s", [D, TOK], BF16, kind="Internal").ap()
    attn_s = nc.dram_tensor("attn_s", [TOK, D], F32, kind="Internal").ap()
    kvloc_s = nc.dram_tensor("kvloc_s", [HEADS, NBLK, DH, DH], F32, kind="Internal").ap()
    D_s = nc.dram_tensor("D_s", [HEADS * DH, DH], F32, kind="Internal").ap()
    Dall_s = nc.dram_tensor("Dall_s", [GROUP * HEADS * DH, DH], F32, kind="Internal").ap()
    acc_s = nc.dram_tensor("acc_s", [TOK, D], F32, kind="Internal").ap()
    kv0_s = nc.dram_tensor("kv0_s", [P, HEADS * DH], F32, kind="Internal").ap()

    dbg_outs = {}
    for name, src in (("q_s", q_s), ("k_s", k_s), ("kt_s", kt_s), ("v_s", v_s),
                      ("gate_s", gate_s), ("h0_s", h0_s), ("h1_s", h1_s),
                      ("attn_s", attn_s), ("D_s", D_s), ("acc_s", acc_s),
                      ("kvloc_s", kvloc_s), ("Dall_s", Dall_s), ("kv0_s", kv0_s)):
        if name in dbg:
            dbg_outs[name] = (nc.dram_tensor("dbg_" + name, list(src.shape),
                                             src.dtype, kind="ExternalOutput").ap(), src)
    if "xn_s" in dbg:
        dbg_outs["xn_s"] = (nc.dram_tensor("dbg_xn_s", [P, NF * TOK], BF16,
                                           kind="ExternalOutput").ap(), None)
    if "stats_s" in dbg:
        dbg_outs["stats_s"] = (nc.dram_tensor("dbg_stats_s", [NT, P, 16], F32,
                                              kind="ExternalOutput").ap(), None)
    if "dup_s" in dbg:
        dbg_outs["dup_s"] = (nc.dram_tensor("dbg_dup_s", [P, 512], F32,
                                            kind="ExternalOutput").ap(), None)
        dbg_outs["dup2_s"] = (nc.dram_tensor("dbg_dup2_s", [P, 512], F32,
                                             kind="ExternalOutput").ap(), None)

    from contextlib import ExitStack
    with ExitStack() as ctx:
        e = ctx.enter_context

        BASE = (nc.sbuf_base + 31) // 32 * 32

        def at(name, off, shape, dtype):
            assert off % 32 == 0, (name, off)
            return nc.alloc_sbuf_tensor_at(name, list(shape), dtype, offset=BASE + off)

        big_bf = at("big_bf", OFF_BIG, [P, 65536], BF16)
        cq = [at(f"cq{i}", OFF_BIG + i * 4 * KB, [P, TOK], BF16) for i in range(2)]
        ck = [at(f"ck{i}", OFF_BIG + 8 * KB + i * 4 * KB, [P, TOK], BF16) for i in range(2)]
        cv = [at(f"cv{i}", OFF_BIG + 16 * KB + i * 4 * KB, [P, TOK], BF16) for i in range(2)]
        ckv = [at(f"ckv{i}", OFF_BIG + 24 * KB + i * 4 * KB, [P, NBLK * DH], F32)
               for i in range(2)]
        cattn = at("cattn", OFF_BIG + 32 * KB, [P, NT * DH], F32)
        cmask = [at(f"cmask{i}", OFF_BIG + 40 * KB + i * 2 * KB, [P, 512], F32)
                 for i in range(2)]
        ckvt = at("ckvt", OFF_BIG + 44 * KB, [P, 2 * DH], F32)
        ckvtb = at("ckvtb", OFF_BIG + 45 * KB, [P, 2 * DH], BF16)
        cintr = at("cintr", OFF_BIG + 46 * KB, [P, 2 * 256], F32)
        csT = at("csT", OFF_BIG + 48 * KB, [P, 2 * 512], BF16)

        xt = at("xt", OFF_B, [P, D], F32)
        xn_tm = at("xn_tm", OFF_B + 16 * KB, [P, D], BF16)
        junk_bf = at("junk_bf", OFF_B + 24 * KB, [P, D], BF16)
        slabA = [at(f"slabA{i}", OFF_B + i * 8 * KB, [P, NF * 128], BF16)
                 for i in range(3)]
        ktst = [at(f"ktst{i}", OFF_B + 32 * KB + i * 4 * KB, [P, TOK], BF16)
                for i in range(2)]
        slabQ = [at(f"slabQ{i}", OFF_B + i * 8 * KB, [P, 8 * 512], BF16)
                 for i in range(5)]
        xacc = [at(f"xacc{i}", OFF_B + 40 * KB + i * 2 * KB, [P, 512], F32)
                for i in range(4)]
        ktb = at("ktb", OFF_B, [P, 2 * HEADS * DH], BF16)
        vtb = at("vtb", OFF_B + 16 * KB, [P, 2 * HEADS * DH], BF16)
        dallb = at("dallb", OFF_B + 32 * KB, [P, HEADS * DH], F32)
        mat = [at(f"mat{i}", OFF_B + i * 8 * KB, [P, TOK], F32) for i in range(2)]
        mgt = [at(f"mgt{i}", OFF_B + 16 * KB + i * 8 * KB, [P, TOK], F32) for i in range(2)]
        mgtm = at("mgtm", OFF_B + 40 * KB, [P, TOK], BF16)

        estf = [at(f"estf{i}", OFF_EST + i * 2 * KB, [P, 512], F32) for i in range(2)]
        estb = [at(f"estb{i}", OFF_EST + 4 * KB + i * KB, [P, 512], BF16) for i in range(4)]

        kv_sb = at("kv_sb", OFF_KV, [P, HEADS * DH], F32)
        rs_row = at("rs_row", OFF_KV, [1, TOK], F32)

        cur = OFF_CONST

        def nxt(name, shape, dtype, nbytes):
            nonlocal cur
            t = at(name, cur, shape, dtype)
            cur += (nbytes + 31) // 32 * 32
            return t

        maskb = nxt("maskb", [P, 512], F32, 2048)
        rqk = nxt("rqk", [P, 4], F32, 16)
        qdall = nxt("qdall", [P, 64], F32, 256)
        kdall = nxt("kdall", [P, 64], F32, 256)
        c2P = nxt("c2P", [P, NF], F32, 128)
        nrmP = nxt("nrmP", [P, NF], F32, 128)
        coefP = nxt("coefP", [P, 128], F32, 512)
        ident = nxt("ident", [P, 128], BF16, 256)
        ones_bf = nxt("ones_bf", [P, 1], BF16, 32)
        onesf = nxt("onesf", [1, 512], F32, 2048)
        scb = nxt("scb", [P, 8], F32, 32)
        stats = nxt("stats", [P, 16], F32, 64)
        sc_sb = nxt("sc_sb", [1, 3], F32, 32)
        slopeneg = nxt("slopeneg", [P, 32], F32, 128)
        miscc = nxt("miscc", [P, 4], F32, 16)
        assert BASE + cur <= nc.sbuf_top, (BASE, cur, nc.sbuf_top)

        pb = [e(nc.psum_tensor(f"pb{i}", [P, 512], F32)) for i in range(6)]
        pt = [e(nc.psum_tensor(f"pt{i}", [P, 512], BF16)) for i in range(2)]

        p = Prog(nc, ctx)
        p._pt0 = pt[0]
        p._ident = ident[:, :]
        p._spare = stats
        p._qd = miscc

        bank_free = {i: [] for i in range(6)}
        pt_free = [[], []]
        estb_free = [[] for _ in range(4)]
        estf_free = [[] for _ in range(2)]
        slabA_free = [0, 0, 0]
        slabQ_free = [0, 0, 0, 0, 0]

        def xn_col(f, t0):
            return f * TOK + t0

        p.emit("sp", lambda e: e.dma_start(out=maskb[:, :], in_=consts_in[:, 0:512]), "s_sp", 16)
        p.emit("sp", lambda e: e.dma_start(out=rqk[:, :], in_=consts_in[:, 512:516]), "s_sp", 16)
        p.emit("sp", lambda e: e.dma_start(out=coefP[:, :], in_=consts_in[:, 516:644]), "s_sp", 16)
        p.emit("sp", lambda e: e.dma_start(out=c2P[:, :], in_=consts_in[:, 644:676]), "s_sp", 16)
        p.emit("sp", lambda e: e.dma_start(out=nrmP[:, :], in_=consts_in[:, 676:708]), "s_sp", 16)
        p.emit("sp", lambda e: e.dma_start(out=sc_sb[:, :], in_=sc_in[:, :]), "s_sp", 16)
        p.emit("sp", lambda e: e.dma_start(out=slopeneg[:, :], in_=consts_in[:, 836:868]), "s_sp", 16)
        p.emit("sp", lambda e: e.dma_start(out=miscc[:, :], in_=consts_in[:, 868:872]), "s_sp", 16)
        n_setup_sp = p.cnt["s_sp"]
        p.emit("gp", lambda e: e.dma_start(out=ident[:, :], in_=consts_in[:, 708:836]), "s_gp", 16)

        p.emit("dve", lambda e: e.memset(ones_bf[:, :], 1.0), "s_dve")
        p.emit("dve", lambda e: e.memset(onesf[:, :], 1.0), "s_dve")

        p.wait("pe", "s_sp", n_setup_sp)
        p.wait("pe", "s_dve", 2)
        p.wait("pe", "s_gp", 16)
        p.spacer("pe")
        p.emit("pe", lambda e: e.matmul(pb[5][:, 0:3], onesf[0:1, 0:128], sc_sb[0:1, 0:3],
                                        start=True, stop=True), "s_pe")
        p.wait("dve", "s_pe", p.cnt["s_pe"])
        n_cp = p.emit("dve", lambda e: e.tensor_copy(
            out=scb[:, 0:3], in_=pb[5][:, 0:3]), "s_dve")
        p.wait("dve", "s_dve", n_cp)
        n_scb = p.emit("dve", lambda e: e.tensor_scalar(
            out=scb[:, 3:6], in0=scb[:, 0:3], scalar1=2.0, scalar2=None, op0=ALU.mult),
            "s_dve")
        bank_free[5] = [("s_dve", n_scb)]

        p.act_warmup(AF.Ln, 33)
        p.wait("act", "s_sp", n_setup_sp)
        for h in range(HEADS):
            p.emit("act", lambda e, h=h: e.activation(
                out=qdall[:, 2 * h:2 * h + 2], in_=rqk[:, 0:2], func=AF.Exp,
                scale=slopeneg[:, h:h + 1]), None)
            p.emit("act", lambda e, h=h: e.activation(
                out=kdall[:, 2 * h:2 * h + 2], in_=rqk[:, 2:4], func=AF.Exp,
                scale=slopeneg[:, h:h + 1]), None)
        n_kd = p.emit("act", lambda e: e.copy(
            out=stats[:, 15:16], in_=stats[:, 14:15]), "s_act")

        def transposes(src_tile, n_sub, dst, col_of, scale_of, sub_off=0):
            nonlocal pt_free
            for g0 in range(0, n_sub, 4):
                gi = (g0 // 4) % 2
                p.wait_multi("pe", pt_free[gi])
                cnt = min(4, n_sub - g0)
                tot = None
                for s in range(g0, g0 + cnt):
                    tot = p.emit("pe", lambda e, s=s, gi=gi: e.transpose(
                        pt[gi][:, (s % 4) * 128:(s % 4) * 128 + 128],
                        src_tile[:, (sub_off + s) * 128:(sub_off + s) * 128 + 128],
                        ident[:, :]), "s_pe" if s == g0 + cnt - 1 else None)
                p.wait("dve", "s_pe", tot)
                t2 = None
                for s in range(g0, g0 + cnt):
                    sc = scale_of(s)
                    last = s == g0 + cnt - 1
                    if sc is None:
                        t2 = p.emit("dve", lambda e, s=s, gi=gi: e.tensor_copy(
                            out=dst[:, col_of(s):col_of(s) + 128],
                            in_=pt[gi][:, (s % 4) * 128:(s % 4) * 128 + 128]),
                            "s_dve" if last else None)
                    else:
                        t2 = p.emit("dve", lambda e, s=s, gi=gi, sc=sc: e.tensor_scalar(
                            out=dst[:, col_of(s):col_of(s) + 128],
                            in0=pt[gi][:, (s % 4) * 128:(s % 4) * 128 + 128],
                            scalar1=sc, scalar2=None, op0=ALU.mult),
                            "s_dve" if last else None)
                pt_free[gi] = [("s_dve", t2)]

        def fm_phase(wsrc, dst, func, scale, corr_wait, tr_to_kt=False, warm=None):
            p.gate()
            if warm is not None:
                p.act_warmup(warm)
            ktst_store = [0, 0]
            for oc in range(NF):
                si = oc % 3
                p.wait("gp", "s_pe", slabA_free[si])
                ld = p.emit("gp", lambda e, oc=oc, si=si: e.dma_start(
                    out=slabA[si][:, :], in_=wsrc[oc]), "s_gp", 16)
                mm_tot = None
                for half in range(2):
                    bpair = ((oc * 2 + half) % 2 * 2, (oc * 2 + half) % 2 * 2 + 1)
                    for b in bpair:
                        p.wait_multi("pe", bank_free[b])
                    p.wait("pe", "s_gp", ld)
                    if oc == 0 and half == 0:
                        p.spacer("pe")
                    if corr_wait is not None:
                        p.wait("pe", "s_act", corr_wait)
                    for f in range(NF):
                        for tg2 in range(2):
                            tg = half * 2 + tg2
                            b = bpair[tg2]
                            p.emit("pe", lambda e, f=f, tg=tg, b=b, si=si: e.matmul(
                                pb[b][:, :],
                                slabA[si][:, f * 128:(f + 1) * 128],
                                big_bf[:, xn_col(f, tg * 512):xn_col(f, tg * 512) + 512],
                                start=(f == 0),
                                stop=(f == NF - 1 and corr_wait is None and tg2 == 1)),
                                "s_pe" if (f == NF - 1 and corr_wait is None and tg2 == 1)
                                else None)
                    if corr_wait is not None:
                        for tg2 in range(2):
                            tg = half * 2 + tg2
                            b = bpair[tg2]
                            mm_tot = p.emit("pe", lambda e, b=b, tg=tg: e.matmul(
                                pb[b][:, :], onesf[0:1, 0:128],
                                rs_row[0:1, tg * 512:(tg + 1) * 512],
                                start=False, stop=True), "s_pe" if tg2 == 1 else None)
                    else:
                        mm_tot = p.cnt["s_pe"]
                    p.wait("act", "s_pe", mm_tot)
                    for tg2 in range(2):
                        tg = half * 2 + tg2
                        b = bpair[tg2]
                        eslot = tg % 4
                        p.wait_multi("act", estb_free[eslot])
                        kwargs = {} if scale is None else {"scale": scale}
                        ev = p.emit("act", lambda e, b=b, eslot=eslot, kw=kwargs: e.activation(
                            out=estb[eslot][:, :], in_=pb[b][:, :], func=func, **kw),
                            "s_act")
                        bank_free[b] = [("s_act", ev)]
                        st = p.emit("act", lambda e, oc=oc, tg=tg, eslot=eslot: e.dma_start(
                            out=dst(oc, tg), in_=estb[eslot][:, :]), "s_actst", 16)
                        estb_free[eslot] = [("s_actst", st)]
                        if tr_to_kt:
                            ksl = oc % 2
                            p.wait("dve", "s_spst", ktst_store[ksl])
                            p.wait("pe", "s_act", ev)
                            gi = tg % 2
                            p.wait_multi("pe", pt_free[gi])
                            tot = None
                            for s in range(4):
                                tot = p.emit("pe", lambda e, s=s, gi=gi, eslot=eslot: e.transpose(
                                    pt[gi][:, s * 128:(s + 1) * 128],
                                    estb[eslot][:, s * 128:(s + 1) * 128],
                                    ident[:, :]), "s_pe" if s == 3 else None)
                            estb_free[eslot].append(("s_pe", tot))
                            p.wait("dve", "s_pe", tot)
                            t2 = None
                            for s in range(4):
                                t0 = tg * 512 + s * 128
                                mh = (t0 // 128) % 2
                                t2 = p.emit("dve", lambda e, s=s, gi=gi, ksl=ksl, t0=t0,
                                            mh=mh, oc=oc: e.tensor_scalar(
                                    out=ktst[ksl][:, t0:t0 + 128],
                                    in0=pt[gi][:, s * 128:(s + 1) * 128],
                                    scalar1=kdall[:, 2 * oc + mh:2 * oc + mh + 1],
                                    scalar2=None, op0=ALU.mult),
                                    "s_dve" if s == 3 else None)
                            pt_free[gi] = [("s_dve", t2)]
                slabA_free[si] = mm_tot
                if tr_to_kt:
                    ksl = oc % 2
                    p.wait("sp", "s_dve", p.cnt["s_dve"])
                    ktst_store[ksl] = p.emit("sp", lambda e, oc=oc, ksl=ksl: e.dma_start(
                        out=kt_s.rearrange("(tt p) h d -> p tt h d", p=128)[:, :, oc, :],
                        in_=ktst[ksl][:, :].rearrange("p (tt d) -> p tt d", d=DH)),
                        "s_spst", 16)

        def tm_phase(wsrc, dst, func, scale, store_f32=False, x_src=None,
                     out_dst=None, corr=False, act_scale_col=None, warm=None):
            p.gate()
            if warm is not None:
                p.act_warmup(warm)
            qn = 0
            lds = {}
            for oc in range(8):
                for qq in range(4):
                    si = qn % 5
                    p.wait("gp", "s_pe", slabQ_free[si])
                    lds[(oc, qq)] = (si, p.emit("gp", lambda e, oc=oc, qq=qq, si=si: e.dma_start(
                        out=slabQ[si][:, :], in_=wsrc[oc, qq]), "s_gp", 16))
                    qn += 1
                for tt in range(NT):
                    b = tt % 4
                    p.wait_multi("pe", bank_free[b])
                    if x_src is not None:
                        xsl = tt % 4
                        p.wait("sp", "s_dve", xacc_guard[xsl])
                        ldx = p.emit("sp", lambda e, oc=oc, tt=tt, xsl=xsl: e.dma_start(
                            out=xacc[xsl][:, :],
                            in_=x_src[tt * 128:(tt + 1) * 128, oc * 512:(oc + 1) * 512]),
                            "s_sp", 16)
                    mm_tot = None
                    for qq in range(4):
                        si, ld = lds[(oc, qq)]
                        p.wait("pe", "s_gp", ld)
                        if oc == 0 and tt == 0 and qq == 0:
                            p.spacer("pe")
                        for f2 in range(8):
                            f = qq * 8 + f2
                            tile_end = (qq == 3 and f2 == 7 and not corr)
                            qq_end = (tt == NT - 1 and f2 == 7 and not tile_end)
                            tot = p.emit("pe", lambda e, f=f, tt=tt, b=b, si=si, f2=f2: e.matmul(
                                pb[b][:, :],
                                big_bf[:, xn_col(f, tt * 128):xn_col(f, tt * 128) + 128],
                                slabQ[si][:, f2 * 512:(f2 + 1) * 512],
                                start=(f == 0), stop=(f == NF - 1 and not corr)),
                                "s_pe" if (tile_end or qq_end) else None)
                            if tile_end:
                                mm_tot = tot
                        if tt == NT - 1:
                            slabQ_free[si] = p.cnt["s_pe"]
                    if corr:
                        mm_tot = p.emit("pe", lambda e, tt=tt, b=b: e.matmul(
                            pb[b][:, :], rs_row[0:1, tt * 128:(tt + 1) * 128],
                            onesf[0:1, 0:512], start=False, stop=True), "s_pe")
                    if x_src is None and out_dst is None:
                        p.wait("act", "s_pe", mm_tot)
                        if store_f32:
                            eslot = tt % 2
                            tb = estf
                            p.wait_multi("act", estf_free[eslot])
                        else:
                            eslot = tt % 4
                            tb = estb
                            p.wait_multi("act", estb_free[eslot])
                        kwargs = {} if scale is None else {"scale": scale}
                        ev = p.emit("act", lambda e, b=b, eslot=eslot, tb=tb, kw=kwargs:
                                    e.activation(out=tb[eslot][:, :], in_=pb[b][:, :],
                                                 func=func, **kw), "s_act")
                        bank_free[b] = [("s_act", ev)]
                        st = p.emit("act", lambda e, oc=oc, tt=tt, eslot=eslot, tb=tb:
                                    e.dma_start(out=dst(oc, tt), in_=tb[eslot][:, :]),
                                    "s_actst", 16)
                        (estf_free if store_f32 else estb_free)[eslot] = [("s_actst", st)]
                    elif not corr:
                        eslot = tt % 2
                        p.wait_multi("dve", estf_free[eslot])
                        p.wait("dve", "s_pe", mm_tot)
                        p.wait("dve", "s_sp", ldx)
                        if oc == 0 and tt == 0:
                            p.spacer("dve")
                        ev = p.emit("dve", lambda e, b=b, xsl=xsl, eslot=eslot: e.tensor_add(
                            out=estf[eslot][:, :], in0=pb[b][:, :], in1=xacc[xsl][:, :]),
                            "s_dve")
                        bank_free[b] = [("s_dve", ev)]
                        xacc_guard[xsl] = ev
                        p.wait("sp", "s_dve", ev)
                        st = p.emit("sp", lambda e, oc=oc, tt=tt, eslot=eslot: e.dma_start(
                            out=out_dst[tt * 128:(tt + 1) * 128, oc * 512:(oc + 1) * 512],
                            in_=estf[eslot][:, :]), "s_spst", 16)
                        estf_free[eslot] = [("s_spst", st)]
                    else:
                        eslot = tt % 2
                        p.wait("act", "s_pe", mm_tot)
                        p.wait_multi("act", estf_free[eslot])
                        ev = p.emit("act", lambda e, b=b, eslot=eslot: e.activation(
                            out=estf[eslot][:, :], in_=pb[b][:, :], func=AF.Identity,
                            scale=scb[:, act_scale_col:act_scale_col + 1]), "s_act")
                        bank_free[b] = [("s_act", ev)]
                        p.wait("dve", "s_act", ev)
                        p.wait("dve", "s_sp", ldx)
                        if oc == 0 and tt == 0:
                            p.spacer("dve")
                        fv = p.emit("dve", lambda e, eslot=eslot, xsl=xsl: e.tensor_add(
                            out=estf[eslot][:, :], in0=estf[eslot][:, :],
                            in1=xacc[xsl][:, :]), "s_dve")
                        xacc_guard[xsl] = fv
                        p.wait("sp", "s_dve", fv)
                        st = p.emit("sp", lambda e, oc=oc, tt=tt, eslot=eslot: e.dma_start(
                            out=out_dst[tt * 128:(tt + 1) * 128, oc * 512:(oc + 1) * 512],
                            in_=estf[eslot][:, :]), "s_out", 16)
                        estf_free[eslot] = [("s_out", st)]

        xacc_guard = [0, 0, 0, 0]

        def rowsum_phase():
            ev = None
            for tg in range(4):
                p.wait_multi("pe", bank_free[4])
                tot = None
                for f in range(NF):
                    tot = p.emit("pe", lambda e, f=f, tg=tg: e.matmul(
                        pb[4][0:1, :], ones_bf[:, 0:1],
                        big_bf[:, xn_col(f, tg * 512):xn_col(f, tg * 512) + 512],
                        start=(f == 0), stop=(f == NF - 1)),
                        "s_pe" if f == NF - 1 else None)
                p.wait("act", "s_pe", tot)
                ev = p.emit("act", lambda e, tg=tg: e.activation(
                    out=rs_row[0:1, tg * 512:(tg + 1) * 512], in_=pb[4][0:1, :],
                    func=AF.Identity, scale=miscc[0:1, 0:1]), "s_act")
                bank_free[4] = [("s_act", ev)]
            return ev

        def finish(last_phase):
            p.gate()
            for name, (dout, src) in dbg_outs.items():
                if name in ("kv0_s", "stats_s", "dup_s", "dup2_s"):
                    continue
                if name == "xn_s":
                    src = big_bf[:, :]
                p.emit("sp", lambda e, dout=dout, src=src: e.dma_start(out=dout, in_=src),
                       "s_out", 16)
            p.wait("sp", "s_out", p.cnt["s_out"])

            with nc.Block() as block:
                @block.tensor
                def _(eng):
                    for fn in p.ops["pe"]:
                        fn(eng)

                @block.scalar
                def _(eng):
                    for fn in p.ops["act"]:
                        fn(eng)

                @block.vector
                def _(eng):
                    for fn in p.ops["dve"]:
                        fn(eng)

                @block.sync
                def _(eng):
                    for fn in p.ops["sp"]:
                        fn(eng)

                @block.gpsimd
                def _(eng):
                    for fn in p.ops["gp"]:
                        fn(eng)
            return nc

        prev_dve = 0
        prev_act = n_kd
        for i in range(NT):
            p.wait("sp", "s_dve", prev_dve)
            p.wait("sp", "s_act", prev_act)
            ld = p.emit("sp", lambda e, i=i: e.dma_start(
                out=xt[:, :], in_=x_in[i * 128:(i + 1) * 128, :]), "s_sp", 16)
            p.wait("dve", "s_sp", ld)
            if i == 0:
                p.spacer("dve")
            rsum = p.emit("dve", lambda e: e.reduce_sum(
                out=stats[:, 0:1], in_=xt[:, :], axis=AX), "s_dve")
            p.wait("act", "s_sp", ld)
            if i == 0:
                p.spacer("act")
            sq = p.emit("act", lambda e: e.activation(
                out=junk_bf[:, :], in_=xt[:, :], func=AF.Square,
                accum_out=stats[:, 1:2]), "s_act")
            p.wait("act", "s_act", sq)
            sqc = p.emit("act", lambda e: e.copy(
                out=stats[:, 9:10], in_=stats[:, 1:2]), "s_act")
            p.wait("dve", "s_dve", rsum)
            s2t = p.emit("dve", lambda e: e.tensor_scalar(
                out=stats[:, 2:3], in0=stats[:, 0:1], scalar1=1.0 / D, scalar2=None,
                op0=ALU.mult), "s_dve")
            p.wait("dve", "s_dve", s2t)
            s4t = p.emit("dve", lambda e: e.tensor_mul(out=stats[:, 4:5], in0=stats[:, 2:3],
                                                       in1=stats[:, 2:3]), "s_dve")
            p.wait("dve", "s_act", sqc)
            p.wait("dve", "s_dve", s4t)
            vtot = p.emit("dve", lambda e: e.scalar_tensor_tensor(
                out=stats[:, 3:4], in0=stats[:, 9:10], scalar=1.0 / D,
                in1=stats[:, 4:5], op0=ALU.mult, op1=ALU.subtract), "s_dve")
            p.wait("dve", "s_dve", vtot)
            vcp = p.emit("dve", lambda e: e.tensor_copy(
                out=stats[:, 7:8], in_=stats[:, 3:4]), "s_dve")
            p.wait("act", "s_dve", vcp)
            p.emit("act", lambda e: e.activation(out=stats[:, 5:6], in_=stats[:, 7:8],
                                                 func=AF.Ln, bias=miscc[:, 1:2]), None)
            rtot = p.emit("act", lambda e: e.activation(
                out=stats[:, 6:7], in_=stats[:, 5:6], func=AF.Exp,
                scale=miscc[:, 0:1]), "s_act")
            p.wait("act", "s_act", rtot)
            rcp = p.emit("act", lambda e: e.copy(
                out=stats[:, 8:9], in_=stats[:, 6:7]), "s_act")
            p.wait("dve", "s_act", rcp)
            xn_tot = p.emit("dve", lambda e: e.tensor_scalar(
                out=xn_tm[:, :], in0=xt[:, :], scalar1=stats[:, 2:3],
                scalar2=stats[:, 8:9], op0=ALU.subtract, op1=ALU.mult), "s_dve")
            if "stats_s" in dbg_outs:
                p.wait("sp", "s_dve", xn_tot)
                p.wait("sp", "s_act", rcp)
                snap_st = p.emit("sp", lambda e, i=i: e.dma_start(
                    out=dbg_outs["stats_s"][0][i], in_=stats[:, :]), "s_out", 16)
                p.wait("dve", "s_out", snap_st)
                p.wait("act", "s_out", snap_st)
            p.wait("pe", "s_dve", xn_tot)
            transposes(xn_tm, NF, big_bf, lambda s, i=i: xn_col(s, i * 128),
                       lambda s: None)
            prev_dve = p.cnt["s_dve"]
            prev_act = p.cnt["s_act"]
        if upto == "p0":
            return finish("p0")

        tm_phase(wv_tm, lambda oc, tt: v_s[tt * 128:(tt + 1) * 128, :, :].rearrange(
            "p h d -> p (h d)")[:, oc * 512:(oc + 1) * 512], AF.Silu, None,
                 warm=AF.Silu)
        if upto == "av":
            return finish("av")
        fm_phase(wk_fm, lambda oc, tg: k_s[oc, :, tg * 512:(tg + 1) * 512],
                 AF.Silu, None, None, tr_to_kt=True)
        if upto == "ak":
            return finish("ak")
        fm_phase(wq_fm, lambda oc, tg: q_s[oc, :, tg * 512:(tg + 1) * 512],
                 AF.Silu, None, None)
        if upto == "aq":
            return finish("aq")
        tm_phase(wg_tm, lambda oc, tt: gate_s[tt * 128:(tt + 1) * 128,
                                              oc * 512:(oc + 1) * 512],
                 AF.Sigmoid, None, store_f32=True, warm=AF.Sigmoid)
        if upto == "agate":
            return finish("agate")
        rs_done = rowsum_phase()
        fm_phase(m0_fm, lambda oc, tg: h0_s[oc * 128:(oc + 1) * 128,
                                            tg * 512:(tg + 1) * 512],
                 AF.Relu, scb[:, 3:4], rs_done)
        if upto == "ffn0":
            return finish("ffn0")

        a_done = dict(p.cnt)

        p.gate()
        p.emit("dve", lambda e: e.memset(kv_sb[:, :], 0.0), "s_dve")
        for j in range(NBLK):
            p.wait("sp", "s_dve", p.cnt["s_dve"])
            snap = p.emit("sp", lambda e, j=j: e.dma_start(
                out=kvloc_s.rearrange("h j p e -> p h j e")[:, :, j, :],
                in_=kv_sb[:, :].rearrange("p (h e) -> p h e", e=DH)), "s_spst", 16)
            ldk = p.emit("sp", lambda e, j=j: e.dma_start(
                out=ktb[:, :].rearrange("p (mh h d) -> p mh h d", mh=2, h=HEADS),
                in_=kt_s[j * BS:(j + 1) * BS].rearrange("(mh p) h d -> p mh h d", p=128)),
                "s_sp", 16)
            ldv = p.emit("sp", lambda e, j=j: e.dma_start(
                out=vtb[:, :].rearrange("p (mh h d) -> p mh h d", mh=2, h=HEADS),
                in_=v_s[j * BS:(j + 1) * BS].rearrange("(mh p) h d -> p mh h d", p=128)),
                "s_sp", 16)
            p.wait("pe", "s_sp", ldv)
            if j == 0:
                p.spacer("pe")
            for h in range(HEADS):
                b = 4 + h % 2
                p.wait_multi("pe", bank_free[b])
                tot = None
                for mh in range(2):
                    tot = p.emit("pe", lambda e, h=h, mh=mh, b=b: e.matmul(
                        pb[b][:, 0:DH],
                        ktb[:, (mh * HEADS + h) * DH:(mh * HEADS + h) * DH + DH],
                        vtb[:, (mh * HEADS + h) * DH:(mh * HEADS + h) * DH + DH],
                        start=(mh == 0), stop=(mh == 1)), "s_pe" if mh == 1 else None)
                p.wait("dve", "s_pe", tot)
                p.wait("dve", "s_spst", snap)
                up = p.emit("dve", lambda e, h=h, b=b: e.scalar_tensor_tensor(
                    out=kv_sb[:, h * DH:(h + 1) * DH], in0=kv_sb[:, h * DH:(h + 1) * DH],
                    scalar=float(bd[h]), in1=pb[b][:, 0:DH],
                    op0=ALU.mult, op1=ALU.add), "s_dve")
                bank_free[b] = [("s_dve", up)]
        p.wait("sp", "s_dve", p.cnt["s_dve"])
        d_store = p.emit("sp", lambda e: e.dma_start(
            out=D_s.rearrange("(h p) e -> p h e", p=128),
            in_=kv_sb[:, :].rearrange("p (h e) -> p h e", e=DH)), "s_spst", 16)
        p.wait("gp", "s_spst", d_store)
        p.emit("gp", lambda e: e.collective_compute(
            "AllGather", ALU.bypass, replica_groups=[[0, 1, 2, 3], [4, 5, 6, 7]],
            ins=[D_s], outs=[Dall_s]), "s_col", 1)
        if upto == "b":
            p.wait("sp", "s_col", 1)
            return finish("b")

        p.wait("sp", "s_actst", a_done["s_actst"])
        p.wait("sp", "s_pe", p.cnt["s_pe"])
        p.wait("sp", "s_dve", p.cnt["s_dve"])
        ld_last = None
        for part in range(8):
            ld_last = p.emit("sp", lambda e, part=part: e.dma_start(
                out=big_bf[:, :].rearrange("p (f t) -> p f t", t=TOK)[
                    :, part * 4:(part + 1) * 4, :],
                in_=h0_s.rearrange("(f p) t -> p f t", p=128)[
                    :, part * 4:(part + 1) * 4, :]), "s_sp", 16)
        ld_last = p.emit("sp", lambda e: e.dma_start(
            out=stats[:, 12:16], in_=consts_in[:, 868:872]), "s_sp", 16)
        p.wait("act", "s_sp", ld_last)
        p.wait("act", "s_spst", d_store)
        p.spacer("act")
        for f in range(NF):
            p.emit("act", lambda e, f=f: e.activation(
                out=big_bf[:, xn_col(f, 0):xn_col(f, 0) + TOK],
                in_=big_bf[:, xn_col(f, 0):xn_col(f, 0) + TOK],
                func=AF.Identity, scale=c2P[:, f:f + 1]), None)
        pres1 = p.emit("act", lambda e: e.copy(
            out=stats[:, 15:16], in_=stats[:, 14:15]), "s_act")
        p.wait("pe", "s_act", pres1)
        rs1_done = rowsum_phase()
        fm_phase(m1_fm, lambda oc, tg: h1_s[oc * 128:(oc + 1) * 128,
                                            tg * 512:(tg + 1) * 512],
                 AF.Relu, scb[:, 4:5], rs1_done)
        ffn1_done = dict(p.cnt)
        if upto == "ffn1":
            p.wait("sp", "s_col", 1)
            return finish("ffn1")

        p.wait("sp", "s_col", 1)
        p.wait("sp", "s_pe", ffn1_done["s_pe"])
        p.wait("dve", "s_pe", ffn1_done["s_pe"])
        for cp in range(GROUP):
            p.wait("sp", "s_dve", p.cnt["s_dve"])
            ld = p.emit("sp", lambda e, cp=cp: e.dma_start(
                out=dallb[:, :].rearrange("p (h e) -> p h e", e=DH),
                in_=Dall_s[cp * HEADS * DH:(cp + 1) * HEADS * DH].rearrange(
                    "(h p) e -> p h e", p=128)), "s_sp", 16)
            p.wait("dve", "s_sp", ld)
            p.wait("dve", "s_dve", p.cnt["s_dve"])
            if cp == 0:
                p.spacer("dve")
            for h in range(HEADS):
                last = h == HEADS - 1
                if cp == 0:
                    p.emit("dve", lambda e, h=h, cp=cp: e.tensor_scalar(
                        out=kv_sb[:, h * DH:(h + 1) * DH],
                        in0=dallb[:, h * DH:(h + 1) * DH],
                        scalar1=coefP[:, cp * 32 + h:cp * 32 + h + 1], scalar2=None,
                        op0=ALU.mult), "s_dve" if last else None)
                else:
                    p.emit("dve", lambda e, h=h, cp=cp: e.scalar_tensor_tensor(
                        out=kv_sb[:, h * DH:(h + 1) * DH],
                        in0=dallb[:, h * DH:(h + 1) * DH],
                        scalar=coefP[:, cp * 32 + h:cp * 32 + h + 1],
                        in1=kv_sb[:, h * DH:(h + 1) * DH],
                        op0=ALU.mult, op1=ALU.add), "s_dve" if last else None)
        if "kv0_s" in dbg_outs:
            p.wait("sp", "s_dve", p.cnt["s_dve"])
            p.emit("sp", lambda e: e.dma_start(out=dbg_outs["kv0_s"][0],
                                               in_=kv_sb[:, :]), "s_out", 16)
        if upto == "kvstart":
            return finish("kvstart")

        p.gate()
        p.act_warmup(AF.Ln)
        chead_guard = [0, 0]
        mask_guard = [0, 0]
        attn_store = 0
        for h in range(HEADS):
            hb = h % 2
            p.wait("sp", "s_dve", chead_guard[hb])
            l1 = p.emit("sp", lambda e, h=h, hb=hb: e.dma_start(
                out=cq[hb][:, :], in_=q_s[h]), "s_sp", 16)
            l2 = p.emit("sp", lambda e, h=h, hb=hb: e.dma_start(
                out=ck[hb][:, :], in_=k_s[h]), "s_sp", 16)
            l3 = p.emit("sp", lambda e, h=h, hb=hb: e.dma_start(
                out=cv[hb][:, :].rearrange("p (tt d) -> p tt d", d=DH),
                in_=v_s.rearrange("(tt p) hh d -> p tt hh d", p=128)[:, :, h, :]),
                "s_sp", 16)
            l4 = p.emit("sp", lambda e, h=h, hb=hb: e.dma_start(
                out=ckv[hb][:, :].rearrange("p (j e) -> p j e", e=DH),
                in_=kvloc_s[h].rearrange("j p e -> p j e")), "s_sp", 16)
            p.wait("act", "s_dve", mask_guard[hb])
            mk = p.emit("act", lambda e, h=h, hb=hb: e.activation(
                out=cmask[hb][:, :], in_=maskb[:, :], func=AF.Exp,
                scale=slopeneg[:, h:h + 1]), "s_act")
            p.wait("dve", "s_spst", attn_store)
            for j in range(NBLK):
                jp = j % 2
                sb_bank = (0, 2)[jp]
                ob_bank = (1, 3)[jp]
                p.wait("dve", "s_sp", l4)
                if h < 2 and j == 0:
                    p.spacer("dve")
                kvt_tot = p.emit("dve", lambda e, h=h, hb=hb, j=j, jp=jp:
                                 e.scalar_tensor_tensor(
                    out=ckvt[:, jp * DH:(jp + 1) * DH],
                    in0=kv_sb[:, h * DH:(h + 1) * DH],
                    scalar=float(bd[h] ** j), in1=ckv[hb][:, j * DH:(j + 1) * DH],
                    op0=ALU.mult, op1=ALU.add), "s_dve")
                p.wait("act", "s_dve", kvt_tot)
                cst = p.emit("act", lambda e, jp=jp: e.copy(
                    out=ckvtb[:, jp * DH:(jp + 1) * DH],
                    in_=ckvt[:, jp * DH:(jp + 1) * DH]), "s_act")
                p.wait_multi("pe", bank_free[sb_bank])
                p.wait("pe", "s_sp", l2)
                if h < 2 and j == 0:
                    p.spacer("pe")
                sc_tot = None
                for mh in range(2):
                    sc_tot = p.emit("pe", lambda e, hb=hb, j=j, mh=mh, sb=sb_bank: e.matmul(
                        pb[sb][:, mh * 256:(mh + 1) * 256],
                        ck[hb][:, j * BS + mh * 128:j * BS + mh * 128 + 128],
                        cq[hb][:, j * BS:(j + 1) * BS],
                        start=True, stop=True), "s_pe" if mh == 1 else None)
                p.wait("dve", "s_pe", sc_tot)
                p.wait("dve", "s_act", mk)
                stm = None
                for mh in range(2):
                    stm = p.emit("dve", lambda e, mh=mh, hb=hb, jp=jp, sb=sb_bank:
                                 e.tensor_tensor(
                        out=csT[:, jp * 512 + mh * 256:jp * 512 + (mh + 1) * 256],
                        in0=pb[sb][:, mh * 256:(mh + 1) * 256],
                        in1=cmask[hb][:, mh * 256:(mh + 1) * 256],
                        op=ALU.mult), "s_dve" if mh == 1 else None)
                bank_free[sb_bank] = [("s_dve", stm)]
                p.wait_multi("pe", bank_free[ob_bank])
                p.wait("pe", "s_dve", stm)
                p.wait("pe", "s_act", cst)
                p.wait("pe", "s_sp", l3)
                out_tot = None
                for nh in range(2):
                    for mh in range(2):
                        p.emit("pe", lambda e, nh=nh, mh=mh, hb=hb, j=j, jp=jp,
                               ob=ob_bank: e.matmul(
                            pb[ob][:, nh * 256:nh * 256 + 128],
                            csT[:, jp * 512 + mh * 256 + nh * 128:
                                jp * 512 + mh * 256 + nh * 128 + 128],
                            cv[hb][:, (2 * j + mh) * DH:(2 * j + mh) * DH + DH],
                            start=(mh == 0), stop=(mh == 1)), None)
                    out_tot = p.emit("pe", lambda e, nh=nh, hb=hb, j=j, jp=jp,
                                     ob=ob_bank: e.matmul(
                        pb[ob][:, nh * 256 + 128:nh * 256 + 256],
                        cq[hb][:, j * BS + nh * 128:j * BS + nh * 128 + 128],
                        ckvtb[:, jp * DH:(jp + 1) * DH],
                        start=True, stop=True), "s_pe" if nh == 1 else None)
                p.wait("act", "s_pe", out_tot)
                iv = None
                for nh in range(2):
                    iv = p.emit("act", lambda e, nh=nh, jp=jp, ob=ob_bank: e.copy(
                        out=cintr[:, jp * 256 + nh * 128:jp * 256 + nh * 128 + 128],
                        in_=pb[ob][:, nh * 256:nh * 256 + 128]),
                        "s_act" if nh == 1 else None)
                p.wait("dve", "s_act", iv)
                mg = None
                for nh in range(2):
                    mg = p.emit("dve", lambda e, nh=nh, h=h, j=j, jp=jp, ob=ob_bank:
                                e.scalar_tensor_tensor(
                        out=cattn[:, (2 * j + nh) * DH:(2 * j + nh) * DH + DH],
                        in0=pb[ob][:, nh * 256 + 128:nh * 256 + 256],
                        scalar=qdall[:, 2 * h + nh:2 * h + nh + 1],
                        in1=cintr[:, jp * 256 + nh * 128:jp * 256 + nh * 128 + 128],
                        op0=ALU.mult, op1=ALU.add), "s_dve" if nh == 1 else None)
                bank_free[ob_bank] = [("s_dve", mg)]
            chead_guard[hb] = p.cnt["s_dve"]
            mask_guard[hb] = p.cnt["s_dve"]
            p.wait("sp", "s_dve", p.cnt["s_dve"])
            attn_store = p.emit("sp", lambda e, h=h: e.dma_start(
                out=attn_s.rearrange("(tt p) f -> p tt f", p=128)[:, :, h * DH:(h + 1) * DH],
                in_=cattn[:, :].rearrange("p (tt d) -> p tt d", d=DH)), "s_spst", 16)
        c_done = dict(p.cnt)
        if upto == "c":
            return finish("c")

        p.gate()
        for i in range(NT):
            lds2 = []
            for half in range(2):
                p.wait("sp", "s_dve", p.cnt["s_dve"])
                la = p.emit("sp", lambda e, i=i, half=half: e.dma_start(
                    out=mat[half][:, :],
                    in_=attn_s[i * 128:(i + 1) * 128, half * TOK:(half + 1) * TOK]),
                    "s_sp", 16)
                lg = p.emit("sp", lambda e, i=i, half=half: e.dma_start(
                    out=mgt[half][:, :],
                    in_=gate_s[i * 128:(i + 1) * 128, half * TOK:(half + 1) * TOK]),
                    "s_sp", 16)
                lds2.append((la, lg))
            sq_tot = None
            for half in range(2):
                p.wait("dve", "s_sp", lds2[half][0])
                if i == 0 and half == 0:
                    p.spacer("dve")
                sq_tot = p.emit("dve", lambda e, half=half: e.scalar_tensor_tensor(
                    out=mgtm[:, :], in0=mat[half][:, :], scalar=1.0,
                    in1=mat[half][:, :], op0=ALU.mult, op1=ALU.mult,
                    accum_out=stats[:, 8 + half:9 + half]), "s_dve")
            p.wait("dve", "s_dve", sq_tot)
            su_tot = p.emit("dve", lambda e: e.tensor_add(
                out=stats[:, 10:11], in0=stats[:, 8:9],
                in1=stats[:, 9:10]), "s_dve")
            p.wait("dve", "s_dve", su_tot)
            ms_tot = p.emit("dve", lambda e: e.tensor_scalar(
                out=stats[:, 11:12], in0=stats[:, 10:11], scalar1=1.0 / D,
                scalar2=None, op0=ALU.mult), "s_dve")
            p.wait("dve", "s_dve", ms_tot)
            msc = p.emit("dve", lambda e: e.tensor_copy(
                out=stats[:, 14:15], in_=stats[:, 11:12]), "s_dve")
            p.wait("act", "s_dve", msc)
            p.emit("act", lambda e: e.activation(out=stats[:, 12:13], in_=stats[:, 14:15],
                                                 func=AF.Ln, bias=miscc[:, 1:2]), None)
            rstd_tot = p.emit("act", lambda e: e.activation(
                out=stats[:, 13:14], in_=stats[:, 12:13], func=AF.Exp,
                scale=miscc[:, 0:1]), "s_act")
            p.wait("act", "s_act", rstd_tot)
            rsc = p.emit("act", lambda e: e.copy(
                out=stats[:, 15:16], in_=stats[:, 13:14]), "s_act")
            for half in range(2):
                p.wait("dve", "s_sp", lds2[half][1])
                p.wait("dve", "s_act", rsc)
                gm = p.emit("dve", lambda e, half=half: e.scalar_tensor_tensor(
                    out=mgtm[:, :], in0=mat[half][:, :], scalar=stats[:, 15:16],
                    in1=mgt[half][:, :], op0=ALU.mult, op1=ALU.mult), "s_dve")
                p.wait("pe", "s_dve", gm)
                transposes(mgtm, 16, big_bf,
                           lambda s, i=i, half=half: xn_col(half * 16 + s, i * 128),
                           lambda s, half=half: nrmP[:, half * 16 + s:half * 16 + s + 1])
        if upto == "merge":
            return finish("merge")

        tm_phase(wo_tm, None, None, None, x_src=x_in, out_dst=acc_s)
        af_done = dict(p.cnt)
        if upto == "afinal":
            return finish("afinal")

        p.wait("sp", "s_pe", af_done["s_pe"])
        p.wait("sp", "s_dve", af_done["s_dve"])
        for part in range(8):
            ld_last = p.emit("sp", lambda e, part=part: e.dma_start(
                out=big_bf[:, :].rearrange("p (f t) -> p f t", t=TOK)[
                    :, part * 4:(part + 1) * 4, :],
                in_=h1_s.rearrange("(f p) t -> p f t", p=128)[
                    :, part * 4:(part + 1) * 4, :]), "s_sp", 16)
        ld_last = p.emit("sp", lambda e: e.dma_start(
            out=stats[:, 12:16], in_=consts_in[:, 868:872]), "s_sp", 16)
        p.wait("act", "s_sp", ld_last)
        p.wait("act", "s_dve", c_done["s_dve"])
        p.spacer("act")
        for f in range(NF):
            p.emit("act", lambda e, f=f: e.activation(
                out=big_bf[:, xn_col(f, 0):xn_col(f, 0) + TOK],
                in_=big_bf[:, xn_col(f, 0):xn_col(f, 0) + TOK],
                func=AF.Identity, scale=c2P[:, f:f + 1]), None)
        pres2 = p.emit("act", lambda e: e.copy(
            out=stats[:, 15:16], in_=stats[:, 14:15]), "s_act")
        p.wait("pe", "s_act", pres2)
        rowsum_phase()
        p.wait("pe", "s_act", p.cnt["s_act"])

        tm_phase(m2_tm, None, None, None, x_src=acc_s, out_dst=out_x, corr=True,
                 act_scale_col=5)

        return finish("ffn2")


def _host_consts():
    slope, bd = _decay_consts()
    maskb = np.zeros((P, 512), np.float32)
    for mh in range(2):
        m_idx = mh * 128 + np.arange(P)[:, None]
        n_idx = np.arange(256)[None, :]
        base = (n_idx - m_idx).astype(np.float32)
        base[n_idx < m_idx] = 50000.0
        maskb[:, mh * 256:(mh + 1) * 256] = base
    rqk = np.zeros((P, 4), np.float32)
    for nh in range(2):
        rqk[:, nh] = nh * 128 + np.arange(P) + 1.0
        rqk[:, 2 + nh] = 255.0 - (nh * 128 + np.arange(P))
    ne = round(D * 0.8)
    cf = np.where(np.arange(D) < ne, 1.0, -np.float32(0.8 / 0.2)).astype(np.float32)
    c2 = (2.0 * cf).reshape(NF, P).T.copy()
    return slope, bd, maskb, rqk, c2


def _pack_fm(w):
    return np.ascontiguousarray(
        w.reshape(NF, P, NF, P).transpose(2, 1, 0, 3).reshape(NF, P, D))


def _pack_tm(w):
    return np.ascontiguousarray(
        w.reshape(4, 8, P, 8, 512).transpose(3, 0, 2, 1, 4).reshape(8, 4, P, D))


_nc_cache = {}


def kernel(L0_attn_hidden_states, L0_attn_qkv_w, L0_attn_out_w, L0_attn_gate_w,
           L0_attn_norm_w, L0_ffn_k0, L0_ffn_s0, L0_ffn_k1, L0_ffn_s1,
           L0_ffn_k2, L0_ffn_s2, _dbg=(), _upto=None):
    x = np.ascontiguousarray(np.asarray(L0_attn_hidden_states, dtype=np.float32))
    qkv_w = np.asarray(L0_attn_qkv_w, dtype=np.float32)
    w3 = qkv_w.reshape(HEADS, 3, DH, D)
    wq_fm = _pack_fm(w3[:, 0].reshape(D, D).T)
    wk_fm = _pack_fm(w3[:, 1].reshape(D, D).T)
    wv_tm = _pack_tm(w3[:, 2].reshape(D, D).T)
    wg_tm = _pack_tm(np.asarray(L0_attn_gate_w, np.float32).T)
    wo_tm = _pack_tm(np.asarray(L0_attn_out_w, np.float32).T)
    norm_w = np.asarray(L0_attn_norm_w, np.float32)
    m0_fm = _pack_fm(np.asarray(L0_ffn_k0).view(np.uint8))
    m1_fm = _pack_fm(np.asarray(L0_ffn_k1).view(np.uint8))
    m2_tm = _pack_tm(np.asarray(L0_ffn_k2).view(np.uint8))
    sc = np.array([[np.float32(L0_ffn_s0), np.float32(L0_ffn_s1),
                    np.float32(L0_ffn_s2)]], np.float32)

    slope, bd, maskb, rqk, c2 = _host_consts()
    nrmP = norm_w.reshape(NF, P).T.copy()

    in_maps = []
    for c in range(NCORE):
        b, qtr = divmod(c, GROUP)
        coefP = np.zeros((P, 128), np.float32)
        for idx in range(GROUP):
            cp = (c // GROUP) * GROUP + idx
            if cp < c:
                for h in range(HEADS):
                    coefP[:, idx * 32 + h] = np.float32(bd[h] ** (NBLK * (c - cp - 1)))
        consts = np.zeros((P, 872), np.float32)
        consts[:, 0:512] = maskb
        consts[:, 512:516] = rqk
        consts[:, 516:644] = coefP
        consts[:, 644:676] = c2
        consts[:, 676:708] = nrmP
        consts[:, 708:836] = np.eye(P, dtype=np.float32)
        consts[:, 836:868] = -slope.astype(np.float32)[None, :]
        consts[:, 868] = -0.5
        consts[:, 869] = 1e-5
        in_maps.append({
            "x_in": np.ascontiguousarray(x[b, qtr * TOK:(qtr + 1) * TOK]),
            "wq_fm": wq_fm, "wk_fm": wk_fm, "wv_tm": wv_tm, "wg_tm": wg_tm,
            "wo_tm": wo_tm, "m0_fm": m0_fm, "m1_fm": m1_fm, "m2_tm": m2_tm,
            "consts": consts, "sc": sc,
        })

    key = (tuple(sorted(_dbg)), _upto)
    if key not in _nc_cache:
        _nc_cache[key] = build_nc(dbg=_dbg, upto=_upto)
    nc = _nc_cache[key]
    res = run_bass_kernel_spmd(nc, in_maps, core_ids=list(range(NCORE)))
    out = np.empty((2, 8192, D), np.float32)
    for c in range(NCORE):
        b, qtr = divmod(c, GROUP)
        out[b, qtr * TOK:(qtr + 1) * TOK] = res.results[c]["out_x"]
    if _dbg or _upto:
        return out, res
    return out
